# version 05 (final)
# baseline (speedup 1.0000x reference)
import numpy as np
import ml_dtypes
from contextlib import ExitStack

import concourse.bass as bass
import concourse.mybir as mybir
from concourse.bass_utils import run_bass_kernel_spmd

F32 = mybir.dt.float32
BF16 = mybir.dt.bfloat16
I32 = mybir.dt.int32
AF = mybir.ActivationFunctionType
ALU = mybir.AluOpType
AXX = mybir.AxisListType.X
bfdt = ml_dtypes.bfloat16

D = 2048
SEQ = 2048
NCORE = 8
TT = 512
NT = SEQ // TT
ND = D // 128
INTER = 5504
NO = INTER // 128
NH = 16
NKV = 4
GH = NH // NKV
VD = 256
EPS = 1.0e-5
NEGBIG = -1.0e30
QSCL = 1.0 / float(np.sqrt(128.0))

FFN_PARTS = {"ln", "gu", "down"}
ATTN_PARTS = {"proj", "qk", "pv", "wo"}
SELF_SEM = {"T": "T", "V": "V", "A": "A"}


class P:

    def __init__(self, es, nc):
        self.nc = nc
        self.es = es
        self.ops = {k: [] for k in ("T", "V", "A", "S")}
        self.sem = {}
        self.cnt = {}
        self.amt = {}
        self.lastw = {}
        for k in ("T", "V", "A"):
            self._mk(k, 1)

    def _mk(self, name, amt):
        self.sem[name] = self.es.enter_context(self.nc.semaphore("sm_" + name))
        self.cnt[name] = 0
        self.amt[name] = amt

    def do(self, eng, fn, inc=None):
        if inc is None:
            self.ops[eng].append(fn)
            return None
        if inc not in self.sem:
            self._mk(inc, 1 if inc in ("T", "V", "A") else 16)
        amt = self.amt[inc]
        self.cnt[inc] += amt
        h = self.sem[inc]
        self.ops[eng].append(lambda e, fn=fn, h=h, a=amt: fn(e).then_inc(h, a))
        return self.cnt[inc]

    def wait(self, eng, dep, val=None):
        if val is None:
            val = self.cnt.get(dep, 0)
        if val is None or val <= 0:
            return
        key = (eng, dep)
        if self.lastw.get(key, 0) >= val:
            return
        self.lastw[key] = val
        h = self.sem[dep]
        self.ops[eng].append(lambda e, h=h, v=val: e.wait_ge(h, v))

    def barrier(self):
        snap = dict(self.cnt)
        for eng in ("T", "V", "A", "S"):
            for dep, v in snap.items():
                if SELF_SEM.get(eng) == dep:
                    continue
                self.wait(eng, dep, v)


class Arena:

    def __init__(self, t, nbytes):
        self.t = t
        self.nbytes = nbytes
        self.off = 0

    def reset(self):
        self.off = 0

    def take(self, cols, dt):
        size = {F32: 4, I32: 4, BF16: 2}[dt]
        self.off = (self.off + 3) & ~3
        b0 = self.off
        self.off += cols * size
        assert self.off <= self.nbytes, f"arena overflow {self.off} > {self.nbytes}"
        v = self.t[:, b0 // 2: (b0 + cols * size) // 2]
        if dt != BF16:
            v = v.bitcast(dt)
        return v


def build_program(do_ffn1=True, do_attn=True, do_ffn2=True,
                  do_mat=None, do_cast=None):
    nc = bass.Bass("TRN2", target_bir_lowering=False, debug=False)
    ap = lambda t: t.ap()
    need_ffn = (do_ffn1 or do_ffn2) if do_mat is None else do_mat

    xT = ap(nc.dram_tensor("xT", [D, SEQ], F32, kind="ExternalInput"))
    wdq = ap(nc.dram_tensor("wdq", [D, 128], F32, kind="ExternalInput"))
    wuqP = ap(nc.dram_tensor("wuqP", [128, NH * 128], F32, kind="ExternalInput"))
    wdkv = ap(nc.dram_tensor("wdkv", [D, 192], F32, kind="ExternalInput"))
    wukv = ap(nc.dram_tensor("wukv", [128, 1280], F32, kind="ExternalInput"))
    wo_in = ap(nc.dram_tensor("wo", [NH * VD, D], F32, kind="ExternalInput"))
    gq_in = ap(nc.dram_tensor("gq", [128, 1], F32, kind="ExternalInput"))
    gkv_in = ap(nc.dram_tensor("gkv", [128, 1], F32, kind="ExternalInput"))
    idxg = ap(nc.dram_tensor("idxg", [D, INTER], I32, kind="ExternalInput"))
    idxu = ap(nc.dram_tensor("idxu", [D, INTER], I32, kind="ExternalInput"))
    idxd = ap(nc.dram_tensor("idxd", [INTER, D], I32, kind="ExternalInput"))
    lut_in = ap(nc.dram_tensor("lutb", [128, 16], F32, kind="ExternalInput"))
    id_in = ap(nc.dram_tensor("ident", [128, 128], BF16, kind="ExternalInput"))
    cm_in = ap(nc.dram_tensor("cmask", [128, 128], F32, kind="ExternalInput"))
    rp_in = ap(nc.dram_tensor("ropes", [128, 4096], F32, kind="ExternalInput"))
    out = ap(nc.dram_tensor("out", [D, SEQ], F32, kind="ExternalOutput"))

    wg_t = ap(nc.dram_tensor("wg_t", [NO, 128, ND, 128], BF16, kind="Internal"))
    wu_t = ap(nc.dram_tensor("wu_t", [NO, 128, ND, 128], BF16, kind="Internal"))
    wd_t = ap(nc.dram_tensor("wd_t", [NO, 128, ND, 128], BF16, kind="Internal"))
    wo_t = ap(nc.dram_tensor("wo_t", [32, 128, D], BF16, kind="Internal"))
    dbgA = dbgB = None
    if "dbg" in ATTN_PARTS:
        dbgA = ap(nc.dram_tensor("dbgA", [128, 8192], BF16,
                                 kind="ExternalOutput"))
        dbgB = ap(nc.dram_tensor("dbgB", [128, 512], F32,
                                 kind="ExternalOutput"))
    v_t = ap(nc.dram_tensor("v_t", [NKV, NT * 4, 128, VD], BF16, kind="Internal"))
    h_t = ap(nc.dram_tensor("h_t", [NO, 128, TT], BF16, kind="Internal"))
    SM_DQ, SM_UQ = 0, ND * 128
    SM_DKV = SM_UQ + NH * 128
    SM_UKV = SM_DKV + ND * 192
    SM_TOT = SM_UKV + 1280
    wsm_t = ap(nc.dram_tensor("wsm_t", [128, SM_TOT], BF16, kind="Internal"))

    with ExitStack() as es:
        p = P(es, nc)

        res = es.enter_context(nc.sbuf_tensor("res", [128, ND * SEQ], F32))
        rows = es.enter_context(nc.sbuf_tensor("rows", [1, 4 * TT], F32))
        R = [rows[0:1, i * TT:(i + 1) * TT] for i in range(4)]
        nm = es.enter_context(nc.sbuf_tensor("nm", [128, 1], F32))
        rs_ = es.enter_context(nc.sbuf_tensor("rs_", [128, 1], F32))
        rr = es.enter_context(nc.sbuf_tensor("rr", [128, 1], F32))
        ident = es.enter_context(nc.sbuf_tensor("identS", [128, 128], BF16))
        cmask = es.enter_context(nc.sbuf_tensor("cmaskS", [128, 128], F32))
        lutb = es.enter_context(nc.sbuf_tensor("lutbS", [128, 16], F32))
        luth = es.enter_context(nc.sbuf_tensor("luthS", [128, 16], F32))
        gq = es.enter_context(nc.sbuf_tensor("gqS", [128, 1], F32))
        gkv = es.enter_context(nc.sbuf_tensor("gkvS", [128, 1], F32))
        ones_b = es.enter_context(nc.sbuf_tensor("ones_b", [128, 1], BF16))
        ones_r = es.enter_context(nc.sbuf_tensor("ones_r", [1, 128], F32))
        ABYTES = 70 * 1024
        AR = Arena(es.enter_context(
            nc.sbuf_tensor("arena", [128, ABYTES // 2], BF16)), ABYTES)

        psA = es.enter_context(nc.psum_tensor("psA", [128, 2048], F32))
        psB = es.enter_context(nc.psum_tensor("psB", [128, 2048], F32))

        stat_sx = psA[0:1, 0:TT]
        stat_sxx = psA[0:1, 512:512 + TT]
        mu_bc = psA[:, 1024:1024 + TT]
        rs_bc = psA[:, 1536:1536 + TT]

        def norm_rows(n):
            inv = 1.0 / float(n)
            p.do('V', lambda e: e.tensor_scalar(out=R[0], in0=stat_sx, scalar1=inv,
                                                scalar2=None, op0=ALU.mult))
            p.do('V', lambda e: e.tensor_scalar(out=R[1], in0=stat_sxx, scalar1=inv,
                                                scalar2=None, op0=ALU.mult))
            p.do('V', lambda e: e.tensor_tensor(out=R[2], in0=R[0], in1=R[0],
                                                op=ALU.mult))
            p.do('V', lambda e: e.tensor_tensor(out=R[1], in0=R[1], in1=R[2],
                                                op=ALU.subtract))
            vV = p.do('V', lambda e: e.tensor_scalar(out=R[3], in0=R[1], scalar1=EPS,
                                                     scalar2=None, op0=ALU.add),
                      inc='V')
            if "nrV" in FFN_PARTS:
                return vV
            p.wait('A', 'V', vV)
            vA = p.do('A', lambda e: e.activation(out=R[2], in_=R[3], func=AF.Sqrt),
                      inc='A')
            p.wait('V', 'A', vA)
            p.do('V', lambda e: e.reciprocal(out=R[1], in_=R[2]))
            p.do('V', lambda e: e.tensor_tensor(out=R[2], in0=R[1], in1=R[1],
                                                op=ALU.mult))
            p.do('V', lambda e: e.tensor_tensor(out=R[2], in0=R[2], in1=R[3],
                                                op=ALU.mult))
            p.do('V', lambda e: e.tensor_scalar(out=R[2], in0=R[2], scalar1=-0.5,
                                                scalar2=1.5, op0=ALU.mult,
                                                op1=ALU.add))
            vN = p.do('V', lambda e: e.tensor_tensor(out=R[1], in0=R[1], in1=R[2],
                                                     op=ALU.mult), inc='V')
            if "nrA" in FFN_PARTS:
                return vN
            p.wait('T', 'V', vN)
            p.do('T', lambda e: e.matmul(out=mu_bc, lhsT=ones_r[:], rhs=R[0],
                                         start=True, stop=True))
            vT = p.do('T', lambda e: e.matmul(out=rs_bc, lhsT=ones_r[:], rhs=R[1],
                                              start=True, stop=True), inc='T')
            p.wait('V', 'T', vT)
            return vT

        def block_ln(s, xln, xb, x2, tf32):
            tSt = [0, 0]
            for k in range(ND):
                rsl = res[:, k * SEQ + s * TT: k * SEQ + s * TT + TT]
                par = k % 2
                xbv = xb[:, par * TT:(par + 1) * TT]
                x2v = x2[:, par * TT:(par + 1) * TT]
                p.wait('V', 'T', tSt[par])
                vV = p.do('V', lambda e, o=xbv, i=rsl: e.tensor_copy(out=o, in_=i),
                          inc='V')
                p.wait('A', 'T', tSt[par])
                vA = p.do('A', lambda e, o=x2v, i=rsl: e.activation(
                    out=o, in_=i, func=AF.Square), inc='A')
                p.wait('T', 'V', vV)
                p.wait('T', 'A', vA)
                st = (k == 0)
                sp = (k == ND - 1)
                p.do('T', lambda e, i=xbv, st=st, sp=sp: e.matmul(
                    out=stat_sx, lhsT=ones_b[:], rhs=i, start=st, stop=sp))
                tSt[par] = p.do('T', lambda e, i=x2v, st=st, sp=sp: e.matmul(
                    out=stat_sxx, lhsT=ones_b[:], rhs=i, start=st, stop=sp),
                    inc='T')
            if "lnS" in FFN_PARTS:
                return p.do('V', lambda e: e.engine_nop(), inc='V')
            p.wait('V', 'T', tSt[(ND - 1) % 2])
            norm_rows(D)
            if "lnN" in FFN_PARTS:
                return p.do('V', lambda e: e.engine_nop(), inc='V')
            mk = None
            for k in range(ND):
                rsl = res[:, k * SEQ + s * TT: k * SEQ + s * TT + TT]
                xo = xln[:, k * TT:(k + 1) * TT]
                p.do('V', lambda e, i=rsl: e.tensor_tensor(out=tf32, in0=i,
                                                           in1=mu_bc,
                                                           op=ALU.subtract))
                mk = p.do('V', lambda e, o=xo: e.tensor_tensor(
                    out=o, in0=tf32, in1=rs_bc, op=ALU.mult), inc='V')
            return mk

        def t5_norm(src_ps, dst, gamma, tProd, xb, x2, tf32):
            xbv = xb[:, 0:TT]
            x2v = x2[:, 0:TT]
            p.wait('V', 'T', tProd)
            vV = p.do('V', lambda e: e.tensor_copy(out=xbv, in_=src_ps), inc='V')
            if "TN1" in ATTN_PARTS:
                return vV
            vC = p.do('V', lambda e: e.tensor_copy(out=tf32, in_=src_ps),
                      inc='V')
            p.wait('A', 'V', vC)
            vA = p.do('A', lambda e: e.activation(out=x2v, in_=tf32,
                                                  func=AF.Square), inc='A')
            if "TN2" in ATTN_PARTS:
                return vV
            p.wait('T', 'V', vV)
            p.wait('T', 'A', vA)
            p.do('T', lambda e: e.matmul(out=stat_sx, lhsT=ones_b[:], rhs=xbv,
                                         start=True, stop=True))
            tSm = p.do('T', lambda e: e.matmul(out=stat_sxx, lhsT=ones_b[:],
                                               rhs=x2v, start=True, stop=True),
                       inc='T')
            if "TN3" in ATTN_PARTS:
                p.wait('V', 'T', tSm)
                return vV
            p.wait('V', 'T', tSm)
            norm_rows(128)
            if "TN4" in ATTN_PARTS:
                return p.do('V', lambda e: e.engine_nop(), inc='V')
            p.do('V', lambda e: e.tensor_tensor(out=tf32, in0=tf32, in1=mu_bc,
                                                op=ALU.subtract))
            p.do('V', lambda e, g=gamma: e.scalar_tensor_tensor(
                out=tf32, in0=tf32, scalar=g[:, 0:1], in1=rs_bc,
                op0=ALU.mult, op1=ALU.mult))
            return p.do('V', lambda e, d=dst: e.tensor_copy(out=d, in_=tf32),
                        inc='V')

        for k in range(ND):
            p.do('S', lambda e, k=k: e.dma_start(
                out=res[:, k * SEQ:(k + 1) * SEQ],
                in_=xT[k * 128:(k + 1) * 128, :]), inc='init')
        p.do('S', lambda e: e.dma_start(out=ident[:], in_=id_in), inc='init')
        p.do('S', lambda e: e.dma_start(out=cmask[:], in_=cm_in), inc='init')
        p.do('S', lambda e: e.dma_start(out=gq[:], in_=gq_in), inc='init')
        p.do('S', lambda e: e.dma_start(out=gkv[:], in_=gkv_in), inc='init')
        p.do('S', lambda e: e.dma_start(out=lutb[:], in_=lut_in), inc='init')
        p.wait('V', 'init')
        p.wait('A', 'init')
        p.do('V', lambda e: e.memset(ones_b[:], 1.0))
        p.do('V', lambda e: e.memset(ones_r[:], 1.0))
        p.do('V', lambda e: e.tensor_scalar(out=luth[:], in0=lutb[:], scalar1=0.5,
                                            scalar2=None, op0=ALU.mult), inc='V')

        if (do_attn if do_cast is None else do_cast):
            AR.reset()
            wtmp = AR.take(3072, F32)
            wcb = AR.take(3072, BF16)
            state = {"copy": 0, "store": 0}

            def cast_small(src, ncols, dst, scale=None):
                if len(src.shape) == 3:
                    dst_tmp = wtmp[:, 0:ncols].rearrange(
                        "p (a b) -> p a b", b=src.shape[-1])
                else:
                    dst_tmp = wtmp[:, 0:ncols]
                p.wait('S', 'V', state["copy"])
                p.do('S', lambda e, s_=src, d_=dst_tmp: e.dma_start(
                    out=d_, in_=s_), inc='wtmpL')
                p.wait('V', 'wtmpL')
                p.wait('V', 'wcbS')
                if scale is None:
                    state["copy"] = p.do('V', lambda e, n=ncols: e.tensor_copy(
                        out=wcb[:, 0:n], in_=wtmp[:, 0:n]), inc='V')
                else:
                    state["copy"] = p.do('V', lambda e, n=ncols, sc=scale:
                                         e.tensor_scalar(
                        out=wcb[:, 0:n], in0=wtmp[:, 0:n], scalar1=sc,
                        scalar2=None, op0=ALU.mult), inc='V')
                p.wait('S', 'V', state["copy"])
                state["store"] = p.do('S', lambda e, d=dst, n=ncols: e.dma_start(
                    out=d, in_=wcb[:, 0:n]), inc='wcbS')

            cast_small(wdq.rearrange("(k p) m -> k p m", p=128).transpose([1, 0, 2]),
                       ND * 128, wsm_t[:, SM_DQ:SM_DQ + ND * 128])
            cast_small(wuqP, NH * 128, wsm_t[:, SM_UQ:SM_UQ + NH * 128],
                       scale=QSCL)
            cast_small(wdkv.rearrange("(k p) m -> k p m", p=128).transpose([1, 0, 2]),
                       ND * 192, wsm_t[:, SM_DKV:SM_DKV + ND * 192])
            cast_small(wukv, 1280, wsm_t[:, SM_UKV:SM_UKV + 1280])
            for j in range(32):
                cast_small(wo_in[j * 128:(j + 1) * 128, :], D, wo_t[j])

        if need_ffn:
            AR.reset()
            p.wait('S', 'wcbS')
            p.wait('S', 'wtmpL')
            idx32 = AR.take(INTER, I32)
            idxbf = AR.take(INTER, BF16)
            acc = AR.take(INTER, BF16)
            tmp = AR.take(INTER, BF16)

            mat_state = {"last_store": 0, "conv": 0}

            def materialize(idx_src, nchunks, width, lut_ap, store_fn):
                last_store = mat_state["last_store"]
                conv = mat_state["conv"]
                for c in range(nchunks):
                    iv = idx32[:, 0:width]
                    bv = idxbf[:, 0:width]
                    av = acc[:, 0:width]
                    tv = tmp[:, 0:width]
                    p.wait('S', 'V', conv)
                    p.do('S', lambda e, c=c, iv=iv: e.dma_start(
                        out=iv, in_=idx_src[c * 128:(c + 1) * 128, :]),
                        inc='idxL')
                    p.wait('V', 'idxL')
                    conv = p.do('V', lambda e, bv=bv, iv=iv: e.tensor_copy(
                        out=bv, in_=iv), inc='V')
                    mat_state["conv"] = conv
                    p.wait('V', 'wmatS')
                    gm = None
                    for k in range(16):
                        sc = lut_ap[:, k:k + 1]
                        if k == 0:
                            p.do('V', lambda e, sc=sc, av=av, bv=bv: e.tensor_scalar(
                                out=av, in0=bv, scalar1=0.0, scalar2=sc,
                                op0=ALU.is_equal, op1=ALU.mult))
                        else:
                            p.do('V', lambda e, sc=sc, k=k, tv=tv, bv=bv:
                                 e.tensor_scalar(out=tv, in0=bv, scalar1=float(k),
                                                 scalar2=sc, op0=ALU.is_equal,
                                                 op1=ALU.mult))
                            gm = p.do('V', lambda e, av=av, tv=tv: e.tensor_tensor(
                                out=av, in0=av, in1=tv, op=ALU.add), inc='V')
                    p.wait('A', 'V', gm)
                    last_store = store_fn(c, av)
                    mat_state["last_store"] = last_store

            def store_gu(dst):
                def f(c, av):
                    last = 0
                    for o0 in range(0, NO, 11):
                        o1 = min(o0 + 11, NO)
                        last = p.do('A', lambda e, c=c, o0=o0, o1=o1, av=av:
                                    e.dma_start(
                                        out=dst[o0:o1, :, c, :]
                                        .transpose([1, 0, 2]),
                                        in_=av.rearrange("p (o m) -> p o m",
                                                         m=128)[:, o0:o1, :]),
                                    inc='wmatS')
                    return last
                return f

            def store_d(c, av):
                return p.do('A', lambda e, c=c, av=av: e.dma_start(
                    out=wd_t[c], in_=av.rearrange("p (d m) -> p d m", m=128)),
                    inc='wmatS')

            materialize(idxg, ND, INTER, lutb, store_gu(wg_t))
            materialize(idxu, ND, INTER, lutb, store_gu(wu_t))
            materialize(idxd, NO, D, luth, store_d)
        p.barrier()

        def emit_ffn():
            AR.reset()
            xln = AR.take(ND * TT, BF16)
            wbg = AR.take(2 * ND * 128, BF16)
            wbu = AR.take(2 * ND * 128, BF16)
            sgf = AR.take(2 * TT, F32)
            hsl = AR.take(2 * TT, BF16)
            wdb = AR.take(2 * 8 * 128, BF16)
            hbf = AR.take(2 * TT, BF16)
            xb = AR.take(2 * TT, BF16)
            x2 = AR.take(2 * TT, BF16)
            tf32 = AR.take(TT, F32)

            gate_ps = [psB[:, 0:512], psB[:, 512:1024]]
            up_ps = [psB[:, 1024:1536], psB[:, 1536:2048]]
            dps = [psA[:, j * 512:(j + 1) * 512] for j in range(4)] + \
                  [psB[:, j * 512:(j + 1) * 512] for j in range(4)]

            for s in range(NT):
                if "ln" not in FFN_PARTS:
                    continue
                xmk = block_ln(s, xln, xb, x2, tf32)
                p.wait('T', 'V', xmk)
                if "gu" not in FFN_PARTS:
                    continue
                consumed = [0, 0]
                evacd = [0, 0]
                hstore = [0, 0]
                for o in range(NO):
                    par = o % 2
                    wgv = wbg[:, par * ND * 128:(par + 1) * ND * 128]
                    wuv = wbu[:, par * ND * 128:(par + 1) * ND * 128]
                    p.wait('S', 'T', consumed[par])
                    p.do('S', lambda e, o=o, d=wgv: e.dma_start(
                        out=d, in_=wg_t[o].rearrange("p d m -> p (d m)")),
                        inc='wbgL%d' % par)
                    p.wait('A', 'T', consumed[par])
                    p.do('A', lambda e, o=o, d=wuv: e.dma_start(
                        out=d, in_=wu_t[o].rearrange("p d m -> p (d m)")),
                        inc='wbuL%d' % par)
                    p.wait('T', 'wbgL%d' % par)
                    p.wait('T', 'wbuL%d' % par)
                    p.wait('T', 'V', evacd[par])
                    tg = None
                    for k in range(ND):
                        tg = p.do('T', lambda e, w=wgv, k=k, par=par,
                                  st=(k == 0), sp=(k == ND - 1): e.matmul(
                            out=gate_ps[par], lhsT=w[:, k * 128:(k + 1) * 128],
                            rhs=xln[:, k * TT:(k + 1) * TT], start=st, stop=sp),
                            inc='T' if k == ND - 1 else None)
                    tu = None
                    for k in range(ND):
                        tu = p.do('T', lambda e, w=wuv, k=k, par=par,
                                  st=(k == 0), sp=(k == ND - 1): e.matmul(
                            out=up_ps[par], lhsT=w[:, k * 128:(k + 1) * 128],
                            rhs=xln[:, k * TT:(k + 1) * TT], start=st, stop=sp),
                            inc='T' if k == ND - 1 else None)
                    consumed[par] = tu
                    sgv = sgf[:, par * TT:(par + 1) * TT]
                    hv = hsl[:, par * TT:(par + 1) * TT]
                    p.wait('A', 'T', tg)
                    p.wait('A', 'V', evacd[par])
                    aS = p.do('A', lambda e, o_=sgv, i=gate_ps[par]: e.activation(
                        out=o_, in_=i, func=AF.Silu), inc='A')
                    p.wait('V', 'T', tu)
                    p.wait('V', 'A', aS)
                    p.wait('V', 'hS%d' % par)
                    vH = p.do('V', lambda e, o_=hv, a=sgv, b=up_ps[par]:
                              e.tensor_tensor(out=o_, in0=a, in1=b, op=ALU.mult),
                              inc='V')
                    evacd[par] = vH
                    p.wait('S', 'V', vH)
                    hstore[par] = p.do('S', lambda e, o=o, i=hv: e.dma_start(
                        out=h_t[o], in_=i), inc='hS%d' % par)
                p.barrier()
                if "down" not in FFN_PARTS:
                    continue
                for half in range(2):
                    cons2 = [0, 0]
                    tD = None
                    for ic in range(NO):
                        par = ic % 2
                        hbv = hbf[:, par * TT:(par + 1) * TT]
                        wdv = wdb[:, par * 8 * 128:(par + 1) * 8 * 128]
                        p.wait('S', 'T', cons2[par])
                        p.do('S', lambda e, ic=ic, d=hbv: e.dma_start(
                            out=d, in_=h_t[ic]), inc='hL%d' % par)
                        p.wait('A', 'T', cons2[par])
                        p.do('A', lambda e, ic=ic, half=half, d=wdv:
                             e.dma_start(out=d, in_=wd_t[
                                 ic, :, half * 8:(half + 1) * 8, :]
                                 .rearrange("p d m -> p (d m)")),
                             inc='wdL%d' % par)
                        p.wait('T', 'hL%d' % par)
                        p.wait('T', 'wdL%d' % par)
                        for j in range(8):
                            tD = p.do('T', lambda e, w=wdv, j=j, h=hbv,
                                      st=(ic == 0), sp=(ic == NO - 1): e.matmul(
                                out=dps[j], lhsT=w[:, j * 128:(j + 1) * 128],
                                rhs=h, start=st, stop=sp),
                                inc='T' if j == 7 else None)
                        cons2[par] = tD
                    p.wait('V', 'T', tD)
                    for j in range(8):
                        dcol = (half * 8 + j) * SEQ + s * TT
                        p.do('V', lambda e, j=j, dcol=dcol: e.tensor_tensor(
                            out=res[:, dcol:dcol + TT],
                            in0=res[:, dcol:dcol + TT],
                            in1=dps[j], op=ALU.add),
                            inc='V' if j == 7 else None)
                    p.barrier()

        def emit_attn():
            AR.reset()
            khead = AR.take(NKV * SEQ, BF16)
            kh = [khead[:, g * SEQ:(g + 1) * SEQ] for g in range(NKV)]
            probs = AR.take(4 * SEQ, BF16)
            xln = probs[:, 0:ND * TT]
            shared2 = AR.take(7168, BF16)
            smuq = AR.take(NH * 128, BF16)
            ropes_f = AR.take(2048, F32)
            cqn = AR.take(TT, BF16)
            ckvn = AR.take(TT, BF16)
            krb = AR.take(TT, BF16)
            krs = AR.take(TT, BF16)
            qb = AR.take(TT, BF16)
            qsh = AR.take(TT, BF16)
            xb = AR.take(2 * TT, BF16)
            x2 = AR.take(2 * TT, BF16)
            tf32 = AR.take(TT, F32)
            xbF = xb.bitcast(F32)
            x2F = x2.bitcast(F32)
            ropC = ropes_f[:, 0:512]
            ropS = ropes_f[:, 512:1024]

            smdq = shared2[:, 0:ND * 128]
            smdkv = shared2[:, ND * 128:ND * 128 + ND * 192]
            smukv = shared2[:, ND * 128 + ND * 192:ND * 128 + ND * 192 + 1280]
            ptsb = shared2[:, 0:512]
            vbuf = [shared2[:, 512 + i * VD:512 + (i + 1) * VD] for i in range(2)]
            outfm = shared2[:, 1024:1024 + 8 * 512]
            wobuf = [shared2[:, 5120 + i * 1024:5120 + (i + 1) * 1024]
                     for i in range(2)]

            if dbgA is not None:
                p.do('S', lambda e: e.dma_start(
                    out=dbgA[:, 3072:3584], in_=khead[:, 0:512]), inc='dbgS')
                p.wait('T', 'dbgS')
            p.do('S', lambda e: e.dma_start(
                out=smuq, in_=wsm_t[:, SM_UQ:SM_UQ + NH * 128]), inc='projL')

            cq_ps = psB[:, 0:512]
            ckv_ps = psB[:, 512:1024]
            kn_ps = psB[0:64, 512:1024]
            kr_ps = psB[0:64, 1024:1536]
            v_ps = psB[:, 1536:1536 + VD]
            q_ps = psB[:, 0:512]
            tr_ps = psB[:, 512:768].bitcast(BF16)
            out_ps = [psB[:, 1024:1536], psB[:, 1536:2048]]
            scores = psA
            dps = [psA[:, j * 512:(j + 1) * 512] for j in range(4)] + \
                  [psB[:, j * 512:(j + 1) * 512] for j in range(4)]

            swapmask = []
            for i in range(0, 32, 2):
                swapmask += [i + 1, i]

            vM_last = 0
            aE_last = 0
            tT_last = 0
            vPT = 0
            tW_last = 0
            vload = [0, 0]

            for s in range(NT):
                T0 = s * TT
                p.barrier()
                p.do('S', lambda e: e.dma_start(
                    out=smdq, in_=wsm_t[:, SM_DQ:SM_DQ + ND * 128]), inc='projL')
                p.do('S', lambda e: e.dma_start(
                    out=smdkv, in_=wsm_t[:, SM_DKV:SM_DKV + ND * 192]),
                    inc='projL')
                p.do('S', lambda e: e.dma_start(
                    out=smukv, in_=wsm_t[:, SM_UKV:SM_UKV + 1280]), inc='projL')
                p.do('S', lambda e, s=s: e.dma_start(
                    out=ropC, in_=rp_in[:, s * TT:(s + 1) * TT]), inc='ropeL')
                p.do('S', lambda e, s=s: e.dma_start(
                    out=ropS, in_=rp_in[:, 2048 + s * TT:2048 + (s + 1) * TT]),
                    inc='ropeL')
                xmk = block_ln(s, xln, xb, x2, tf32)
                p.wait('T', 'projL')
                p.wait('T', 'V', xmk)
                tCQ = None
                for k in range(ND):
                    tCQ = p.do('T', lambda e, k=k, st=(k == 0), sp=(k == ND - 1):
                               e.matmul(out=cq_ps,
                                        lhsT=smdq[:, k * 128:(k + 1) * 128],
                                        rhs=xln[:, k * TT:(k + 1) * TT],
                                        start=st, stop=sp),
                               inc='T' if k == ND - 1 else None)
                if "not5" in ATTN_PARTS:
                    vq5 = 0
                else:
                    vq5 = t5_norm(cq_ps, cqn[:, 0:TT], gq, tCQ, xb, x2, tf32)
                if "proj2" not in ATTN_PARTS and "qk" not in ATTN_PARTS:
                    continue
                for k in range(ND):
                    p.do('T', lambda e, k=k, st=(k == 0), sp=(k == ND - 1):
                         e.matmul(out=ckv_ps,
                                  lhsT=smdkv[:, k * 192:k * 192 + 128],
                                  rhs=xln[:, k * TT:(k + 1) * TT],
                                  start=st, stop=sp))
                tKR = None
                for k in range(ND):
                    tKR = p.do('T', lambda e, k=k, st=(k == 0), sp=(k == ND - 1):
                               e.matmul(out=kr_ps,
                                        lhsT=smdkv[:, k * 192 + 128:(k + 1) * 192],
                                        rhs=xln[:, k * TT:(k + 1) * TT],
                                        start=st, stop=sp),
                               inc='T' if k == ND - 1 else None)
                vk5 = t5_norm(ckv_ps, ckvn[:, 0:TT], gkv, tKR, xb, x2, tf32)
                if "proj3" not in ATTN_PARTS and "qk" not in ATTN_PARTS:
                    continue
                p.wait('V', 'T', tKR)
                p.wait('V', 'ropeL')
                p.do('V', lambda e: e.tensor_copy(out=xbF[0:64, :], in_=kr_ps))
                p.do('V', lambda e: e.stream_shuffle(out=x2F[0:64, :],
                                                     in_=xbF[0:64, :],
                                                     mask=swapmask))
                p.do('V', lambda e: e.tensor_tensor(
                    out=xbF[0:64, :], in0=xbF[0:64, :], in1=ropC[0:64, :],
                    op=ALU.mult))
                p.do('V', lambda e: e.tensor_tensor(
                    out=x2F[0:64, :], in0=x2F[0:64, :], in1=ropS[0:64, :],
                    op=ALU.mult))
                vK = p.do('V', lambda e: e.tensor_tensor(
                    out=krb[0:64, :], in0=xbF[0:64, :], in1=x2F[0:64, :],
                    op=ALU.add), inc='V')
                p.wait('S', 'V', vK)
                for g in range(NKV):
                    p.do('S', lambda e, g=g, T0=T0: e.dma_start(
                        out=kh[g][64:128, T0:T0 + TT], in_=krb[0:64, :]),
                        inc='krD')
                if "proj4" not in ATTN_PARTS and "qk" not in ATTN_PARTS:
                    continue
                vstore = 0
                vE = 0
                for g in range(NKV):
                    p.wait('T', 'V', vk5)
                    p.wait('T', 'V', vE)
                    tK = p.do('T', lambda e, g=g: e.matmul(
                        out=kn_ps, lhsT=smukv[:, g * 64:(g + 1) * 64],
                        rhs=ckvn[:, 0:TT], start=True, stop=True), inc='T')
                    p.wait('V', 'T', tK)
                    vE = p.do('V', lambda e, g=g, T0=T0: e.tensor_copy(
                        out=kh[g][0:64, T0:T0 + TT], in_=kn_ps), inc='V')
                    if dbgA is not None and s == 0 and g == 0:
                        p.wait('S', 'V', vE)
                        dmk2 = p.do('S', lambda e: e.dma_start(
                            out=dbgA[:, 3584:4096], in_=kh[0][:, 0:512]),
                            inc='dS')
                        p.wait('T', 'dS', dmk2)
                    for sub in range(4):
                        kt = s * 4 + sub
                        p.wait('T', 'V', vE)
                        tV = p.do('T', lambda e, g=g, sub=sub: e.matmul(
                            out=v_ps, lhsT=ckvn[:, sub * 128:(sub + 1) * 128],
                            rhs=smukv[:, 256 + g * VD:256 + (g + 1) * VD],
                            start=True, stop=True), inc='T')
                        p.wait('V', 'T', tV)
                        p.wait('V', 'vtS')
                        vE = p.do('V', lambda e: e.tensor_copy(
                            out=krs[:, 0:VD], in_=v_ps), inc='V')
                        p.wait('S', 'V', vE)
                        vstore = p.do('S', lambda e, g=g, kt=kt: e.dma_start(
                            out=v_t[g, kt], in_=krs[:, 0:VD]), inc='vtS')
                p.wait('T', 'krD')
                p.wait('S', 'vtS')
                if dbgA is not None and s == 0:
                    p.wait('S', 'V', p.cnt['V'])
                    p.do('S', lambda e: e.dma_start(
                        out=dbgA[:, 2048:2560], in_=kh[0][:, 0:512]), inc='dS')
                    dmk0 = p.do('S', lambda e: e.dma_start(
                        out=dbgA[:, 2560:3072], in_=krb[:, 0:512]), inc='dS')
                    p.wait('T', 'dS', dmk0)
                nkt = 4 * (s + 1)
                if "qk" not in ATTN_PARTS:
                    continue
                for g in range(NKV):
                    for r in range(GH):
                        hd = g * GH + r
                        p.wait('T', 'V', p.cnt['V'])
                        tQ = p.do('T', lambda e, hd=hd: e.matmul(
                            out=q_ps, lhsT=smuq[:, hd * 128:(hd + 1) * 128],
                            rhs=cqn[:, 0:TT], start=True, stop=True), inc='T')
                        p.wait('V', 'T', tQ)
                        p.do('V', lambda e: e.tensor_copy(
                            out=xbF[:, 0:TT], in_=q_ps))
                        p.do('V', lambda e: e.stream_shuffle(
                            out=x2F[96:128, :], in_=xbF[96:128, :],
                            mask=swapmask))
                        p.do('V', lambda e: e.tensor_tensor(
                            out=xbF[96:128, :], in0=xbF[96:128, :],
                            in1=ropC[96:128, :], op=ALU.mult))
                        p.do('V', lambda e: e.tensor_tensor(
                            out=x2F[96:128, :], in0=x2F[96:128, :],
                            in1=ropS[96:128, :], op=ALU.mult))
                        p.do('V', lambda e: e.tensor_copy(
                            out=qb[0:96, :], in_=xbF[0:96, :]))
                        vQ = p.do('V', lambda e: e.tensor_tensor(
                            out=qb[96:128, :], in0=xbF[96:128, :],
                            in1=x2F[96:128, :], op=ALU.add), inc='V')
                        p.wait('T', 'V', vQ)
                        for qt in range(4):
                            Q = s * 4 + qt
                            kend = (Q + 1) * 128
                            p.wait('T', 'V', vM_last)
                            p.wait('T', 'A', aE_last)
                            tS = None
                            for c0 in range(0, kend, 512):
                                c1 = min(c0 + 512, kend)
                                tS = p.do('T', lambda e, qt=qt, g=g, c0=c0,
                                          c1=c1: e.matmul(
                                    out=scores[:, c0:c1],
                                    lhsT=qb[:, qt * 128:(qt + 1) * 128],
                                    rhs=kh[g][:, c0:c1],
                                    start=True, stop=True),
                                    inc='T' if c1 == kend else None)
                            p.wait('V', 'T', tS)
                            p.do('V', lambda e, Q=Q: e.tensor_tensor(
                                out=scores[:, Q * 128:Q * 128 + 128],
                                in0=scores[:, Q * 128:Q * 128 + 128],
                                in1=cmask[:], op=ALU.add))
                            vM = p.do('V', lambda e, kend=kend: e.reduce_max(
                                out=nm[:], in_=scores[:, 0:kend], axis=AXX,
                                negate=True), inc='V')
                            p.wait('A', 'V', vM)
                            if dbgB is not None and (s, g, r, qt) == (0, 0, 0, 0):
                                vDsc = p.do('V', lambda e: e.tensor_copy(
                                    out=tf32[:, 0:128], in_=scores[:, 0:128]),
                                    inc='V')
                                p.wait('S', 'V', vDsc)
                                p.do('S', lambda e: e.dma_start(
                                    out=dbgB[:, 0:128], in_=tf32[:, 0:128]),
                                    inc='dS')
                                dmkS = p.do('S', lambda e: e.dma_start(
                                    out=dbgB[:, 128:129], in_=nm[:],
                                    allow_slow_non_contiguous=True), inc='dS')
                                p.wait('T', 'dS', dmkS)
                                p.wait('A', 'dS', dmkS)
                                p.wait('V', 'dS', dmkS)
                            p.wait('A', 'T', tT_last)
                            pr = probs[:, qt * SEQ:qt * SEQ + kend]
                            aE = p.do('A', lambda e, pr=pr, kend=kend:
                                      e.activation(out=pr,
                                                   in_=scores[:, 0:kend],
                                                   func=AF.Exp,
                                                   bias=nm[:, 0:1], scale=1.0,
                                                   accum_out=rs_[:, 0:1]),
                                      inc='A')
                            p.wait('V', 'A', aE)
                            if dbgB is not None and (s, g, r, qt) == (0, 0, 0, 0):
                                p.wait('S', 'A', aE)
                                dmkE = p.do('S', lambda e: e.dma_start(
                                    out=dbgA[:, 1536:1664],
                                    in_=probs[:, 0:128]), inc='dS')
                                p.wait('V', 'dS', dmkE)
                            vR = p.do('V', lambda e: e.reciprocal(
                                out=rr[:], in_=rs_[:]), inc='V')
                            p.wait('A', 'V', vR)
                            aN = p.do('A', lambda e, pr=pr: e.mul(
                                out=pr, in_=pr, mul=rr[:, 0:1]), inc='A')
                            if kend < nkt * 128:
                                p.do('V', lambda e, qt=qt, kend=kend,
                                     nkt=nkt:
                                     e.memset(probs[:, qt * SEQ + kend:
                                                    qt * SEQ + nkt * 128],
                                              0.0), inc='V')
                            vM_last, aE_last = vM, aN
                            if dbgA is not None and (s, g, r, qt) == (0, 0, 0, 3):
                                p.wait('S', 'V', p.cnt['V'])
                                p.wait('S', 'A', p.cnt['A'])
                                p.do('S', lambda e: e.dma_start(
                                    out=dbgA[:, 0:512], in_=probs[:, 0:512]),
                                    inc='dS')
                                p.do('S', lambda e: e.dma_start(
                                    out=dbgA[:, 512:1024],
                                    in_=probs[:, SEQ:SEQ + 512]),
                                    inc='dS')
                                p.do('S', lambda e: e.dma_start(
                                    out=dbgA[:, 1024:1536], in_=kh[0][:, 0:512]),
                                    inc='dS')
                                p.do('S', lambda e: e.dma_start(
                                    out=dbgB[:, 0:1], in_=rr[:],
                                    allow_slow_non_contiguous=True), inc='dS')
                                p.do('S', lambda e: e.dma_start(
                                    out=dbgB[:, 1:2], in_=rs_[:],
                                    allow_slow_non_contiguous=True), inc='dS')
                                dmk = p.do('S', lambda e: e.dma_start(
                                    out=dbgB[:, 2:3], in_=nm[:],
                                    allow_slow_non_contiguous=True), inc='dS')
                                p.wait('T', 'dS', dmk)
                        if "pv" not in ATTN_PARTS:
                            continue
                        tPV = None
                        for kt in range(nkt):
                            par = kt % 2
                            p.wait('S', 'T', vload[par])
                            p.do('S', lambda e, g=g, kt=kt, d=vbuf[par]:
                                 e.dma_start(out=d, in_=v_t[g, kt]),
                                 inc='vbL%d' % par)
                            p.wait('T', 'V', p.cnt['V'])
                            p.wait('T', 'A', p.cnt['A'])
                            tT = None
                            for qt in range(4):
                                tT = p.do('T', lambda e, qt=qt, kt=kt:
                                          e.transpose(
                                    out=tr_ps[:, qt * 128:(qt + 1) * 128],
                                    in_=probs[:, qt * SEQ + kt * 128:
                                              qt * SEQ + (kt + 1) * 128],
                                    identity=ident[:]),
                                    inc='T' if qt == 3 else None)
                            tT_last = tT
                            p.wait('V', 'T', tT)
                            vPT = p.do('V', lambda e: e.tensor_copy(
                                out=ptsb, in_=tr_ps), inc='V')
                            p.wait('T', 'V', vPT)
                            p.wait('T', 'vbL%d' % par)
                            for half in range(2):
                                tPV = p.do('T', lambda e, par=par, half=half,
                                           st=(kt == 0), sp=(kt == nkt - 1):
                                           e.matmul(
                                    out=out_ps[half],
                                    lhsT=vbuf[par][:, half * 128:
                                                   (half + 1) * 128],
                                    rhs=ptsb, start=st, stop=sp),
                                    inc='T' if half == 1 else None)
                            vload[par] = tPV
                        p.wait('V', 'T', tPV)
                        p.wait('V', 'T', tW_last)
                        for half in range(2):
                            p.do('V', lambda e, r=r, half=half: e.tensor_copy(
                                out=outfm[:, (r * 2 + half) * 512:
                                          (r * 2 + half + 1) * 512],
                                in_=out_ps[half]),
                                inc='V' if half == 1 else None)
                    if "wo" not in ATTN_PARTS:
                        continue
                    p.wait('T', 'V', p.cnt['V'])
                    p.wait('T', 'A', aE_last)
                    woload = [0, 0]
                    for half in range(2):
                        tW = None
                        for hv in range(8):
                            par = hv % 2
                            p.wait('A', 'T', woload[par])
                            p.do('A', lambda e, g=g, hv=hv, half=half,
                                 d=wobuf[par]: e.dma_start(
                                out=d, in_=wo_t[g * 8 + hv][:,
                                                            half * 1024:
                                                            (half + 1) * 1024]),
                                inc='woL%d' % par)
                            p.wait('T', 'woL%d' % par)
                            for j in range(8):
                                tW = p.do('T', lambda e, par=par, hv=hv, j=j,
                                          st=(hv == 0), sp=(hv == 7): e.matmul(
                                    out=dps[j],
                                    lhsT=wobuf[par][:, j * 128:(j + 1) * 128],
                                    rhs=outfm[:, hv * 512:(hv + 1) * 512],
                                    start=st, stop=sp),
                                    inc='T' if j == 7 else None)
                            woload[par] = tW
                        tW_last = tW
                        p.wait('V', 'T', tW)
                        for j in range(8):
                            dcol = (half * 8 + j) * SEQ + T0
                            p.do('V', lambda e, j=j, dcol=dcol: e.tensor_tensor(
                                out=res[:, dcol:dcol + TT],
                                in0=res[:, dcol:dcol + TT],
                                in1=dps[j], op=ALU.add),
                                inc='V' if j == 7 else None)
                        p.wait('T', 'V', p.cnt['V'])

        if do_ffn1:
            emit_ffn()
            p.barrier()
        if do_attn:
            emit_attn()
            p.barrier()
        if do_ffn2:
            emit_ffn()
            p.barrier()

        for k in range(ND):
            p.do('S', lambda e, k=k: e.dma_start(
                out=out[k * 128:(k + 1) * 128, :],
                in_=res[:, k * SEQ:(k + 1) * SEQ]), inc='outS')
        p.wait('S', 'outS')

        with nc.Block() as block:
            @block.tensor
            def _(eng):
                for f in p.ops['T']:
                    f(eng)

            @block.vector
            def _(eng):
                for f in p.ops['V']:
                    f(eng)

            @block.scalar
            def _(eng):
                for f in p.ops['A']:
                    f(eng)

            @block.sync
            def _(eng):
                for f in p.ops['S']:
                    f(eng)

    return nc


def _prep_inputs(inp):
    f32 = np.float32
    x = np.asarray(inp["L0_attn_x"], dtype=f32)
    w_uq = np.asarray(inp["L0_attn_w_uq"], dtype=f32)
    cols = []
    for h in range(NH):
        cols.append(w_uq[:, h * 96:(h + 1) * 96])
        cols.append(w_uq[:, 1536 + h * 32:1536 + (h + 1) * 32])
    wuqP = np.ascontiguousarray(np.concatenate(cols, axis=1))

    lut = np.asarray(inp["L0_ffn_lut"], dtype=f32)
    lutb = np.ascontiguousarray(np.broadcast_to(lut.reshape(1, 16), (128, 16)))

    ident = np.eye(128, dtype=bfdt)
    cmask = np.where(np.triu(np.ones((128, 128), dtype=bool), k=1),
                     np.float32(NEGBIG), np.float32(0.0)).astype(f32)

    t = np.arange(SEQ, dtype=np.float64)
    ropes = np.zeros((128, 4096), dtype=np.float64)
    invk = 1.0 / (10000.0 ** (np.arange(0, 64, 2) / 64.0))
    angk = invk[:, None] * t[None, :]
    ropes[0:64:2, 0:2048] = np.cos(angk)
    ropes[1:64:2, 0:2048] = np.cos(angk)
    ropes[0:64:2, 2048:4096] = -np.sin(angk)
    ropes[1:64:2, 2048:4096] = np.sin(angk)
    invq = 1.0 / (10000.0 ** (np.arange(0, 32, 2) / 32.0))
    angq = invq[:, None] * t[None, :]
    ropes[96:128:2, 0:2048] = np.cos(angq)
    ropes[97:128:2, 0:2048] = np.cos(angq)
    ropes[96:128:2, 2048:4096] = -np.sin(angq)
    ropes[97:128:2, 2048:4096] = np.sin(angq)
    ropes = ropes.astype(np.float32)

    shared = dict(
        wdq=np.ascontiguousarray(np.asarray(inp["L0_attn_w_dq"], dtype=f32)),
        wuqP=wuqP,
        wdkv=np.ascontiguousarray(np.asarray(inp["L0_attn_w_dkv"], dtype=f32)),
        wukv=np.ascontiguousarray(np.asarray(inp["L0_attn_w_ukv"], dtype=f32)),
        wo=np.ascontiguousarray(np.asarray(inp["L0_attn_w_o"], dtype=f32)),
        gq=np.ascontiguousarray(
            np.asarray(inp["L0_attn_gamma_q"], dtype=f32).reshape(128, 1)),
        gkv=np.ascontiguousarray(
            np.asarray(inp["L0_attn_gamma_kv"], dtype=f32).reshape(128, 1)),
        idxg=np.ascontiguousarray(
            np.asarray(inp["L0_ffn_gate_idx"], dtype=np.int32).T),
        idxu=np.ascontiguousarray(
            np.asarray(inp["L0_ffn_up_idx"], dtype=np.int32).T),
        idxd=np.ascontiguousarray(
            np.asarray(inp["L0_ffn_down_idx"], dtype=np.int32).T),
        lutb=lutb, ident=ident, cmask=cmask, ropes=ropes,
    )
    in_maps = []
    for c in range(NCORE):
        m = dict(shared)
        m["xT"] = np.ascontiguousarray(x[c].T)
        in_maps.append(m)
    return in_maps


def kernel(**inp):
    in_maps = _prep_inputs(inp)
    nc = build_program()
    res = run_bass_kernel_spmd(nc, in_maps, core_ids=list(range(NCORE)))
    outs = [np.asarray(r["out"]).T for r in res.results]
    return np.ascontiguousarray(np.stack(outs, axis=0), dtype=np.float32)
